# version 07 (final)
# baseline (speedup 1.0000x reference)
import numpy as np
import ml_dtypes
from contextlib import ExitStack

import concourse.bass as bass
import concourse.mybir as mybir
from concourse.bass_utils import run_bass_kernel_spmd

F32 = mybir.dt.float32
BF16 = mybir.dt.bfloat16
AF = mybir.ActivationFunctionType
ALU = mybir.AluOpType
AX = mybir.AxisListType

NCORE = 8
T = 2048
C = 1024
FF = 4096
NQ = 4
TQ = T // NQ
NG = T // 128
EPS = 1e-5
NW = FF * C

DEBUG = False
DEBUG_X2 = False
DEBUG_FFN = False
SKIP_PASSB = False
PASSB_LEVEL = 4
SIGN_AS_COPY = False


class _Trunc(Exception):
    pass


class Sem:
    def __init__(self, h):
        self.h = h
        self.n = 0


class Prog:
    ENGINES = ("sync", "scalar", "vector", "tensor", "gpsimd")

    def __init__(self):
        self.q = {e: [] for e in self.ENGINES}
        self.dve_sem = None

    def on(self, eng, fn):
        self.q[eng].append(fn)

    def wait(self, eng, sem, val):
        assert val <= sem.n, f"wait on future value {val} > {sem.n} emitted so far"
        self.q[eng].append(lambda E, s=sem.h, v=val: E.wait_ge(s, v))

    def dma(self, eng, sem, out, in_, **kw):
        def f(E, o=out, i=in_, s=sem.h, kw=kw):
            E.dma_start(out=o, in_=i, **kw).then_inc(s, 16)
        self.q[eng].append(f)
        sem.n += 16
        return sem.n

    SERIALIZE_ACT = False
    SERIALIZE_DVE = True

    def op(self, eng, sem, build, k=1):
        def f(E, b=build, s=sem.h, kk=k):
            b(E).then_inc(s, kk)
        self.q[eng].append(f)
        sem.n += k
        if Prog.SERIALIZE_ACT and eng == "scalar":
            self.q[eng].append(lambda E, s=sem.h, v=sem.n: E.wait_ge(s, v))
        if Prog.SERIALIZE_DVE and eng == "vector":
            self.q[eng].append(lambda E, s=sem.h, v=sem.n: E.wait_ge(s, v))
        return sem.n

    def op0(self, eng, build):
        if Prog.SERIALIZE_DVE and eng == "vector" and self.dve_sem is not None:
            return self.op(eng, self.dve_sem, build)
        self.q[eng].append(build)


def build_program(stage=5):
    nc = bass.Bass("TRN2", target_bir_lowering=False,
                   detect_race_conditions=False)

    x_d = nc.dram_tensor("x", [T, C], F32, kind="ExternalInput").ap()
    wk_d = nc.dram_tensor("wk", [C, C], F32, kind="ExternalInput").ap()
    wv_d = nc.dram_tensor("wv", [C, C], F32, kind="ExternalInput").ap()
    wr_d = nc.dram_tensor("wr", [C, C], F32, kind="ExternalInput").ap()
    wo_d = nc.dram_tensor("wo", [C, C], F32, kind="ExternalInput").ap()
    w1_d = nc.dram_tensor("w1", [FF, C], F32, kind="ExternalInput").ap()
    w2_d = nc.dram_tensor("w2", [C, FF], F32, kind="ExternalInput").ap()
    b1_d = nc.dram_tensor("b1", [1, FF], F32, kind="ExternalInput").ap()
    b2_d = nc.dram_tensor("b2", [1, C], F32, kind="ExternalInput").ap()
    tdec_d = nc.dram_tensor("tdec", [1, C], F32, kind="ExternalInput").ap()
    tfir_d = nc.dram_tensor("tfir", [1, C], F32, kind="ExternalInput").ap()
    gam_d = nc.dram_tensor("gam", [1, C], F32, kind="ExternalInput").ap()
    bet_d = nc.dram_tensor("bet", [1, C], F32, kind="ExternalInput").ap()
    tvec_d = nc.dram_tensor("tvec", [1, T], F32, kind="ExternalInput").ap()
    mask_d = nc.dram_tensor("mask", [1, 8], F32, kind="ExternalInput").ap()
    cvec_d = nc.dram_tensor("cvec", [1, 12], F32, kind="ExternalInput").ap()
    idb_d = nc.dram_tensor("idb", [128, 128], BF16, kind="ExternalInput").ap()
    out_d = nc.dram_tensor("out", [T, C], F32, kind="ExternalOutput").ap()
    dbg = {}
    if DEBUG_X2:
        dbg["x2"] = nc.dram_tensor("dbg_x2", [NG * 128, C], F32,
                                   kind="ExternalOutput").ap()
    if DEBUG_FFN:
        dbg["y"] = nc.dram_tensor("dbg_y", [128, 16 * C], F32,
                                  kind="ExternalOutput").ap()
        dbg["x2r"] = nc.dram_tensor("dbg_x2r", [128, 16 * C], F32,
                                    kind="ExternalOutput").ap()
        dbg["u4"] = nc.dram_tensor("dbg_u4", [128, 16 * C], F32,
                                   kind="ExternalOutput").ap()
        dbg["h"] = nc.dram_tensor("dbg_h", [128, 8 * 8192], BF16,
                                  kind="ExternalOutput").ap()
        dbg["wq2"] = nc.dram_tensor("dbg_wq2", [128, 32 * C], BF16,
                                    kind="ExternalOutput").ap()
        dbg["wq1"] = nc.dram_tensor("dbg_wq1", [128, 8 * FF], BF16,
                                    kind="ExternalOutput").ap()
        dbg["thr"] = nc.dram_tensor("dbg_thr", [1, 4], F32,
                                    kind="ExternalOutput").ap()
    if DEBUG:
        dbg["cols"] = nc.dram_tensor("dbg_cols", [128, 312], F32, kind="ExternalOutput").ap()
        dbg["carry"] = nc.dram_tensor("dbg_carry", [128, 128], F32, kind="ExternalOutput").ap()
        dbg["part"] = nc.dram_tensor("dbg_part", [128, 8 * T], F32, kind="ExternalOutput").ap()
        dbg["z"] = nc.dram_tensor("dbg_z", [128, 8 * T], BF16, kind="ExternalOutput").ap()
        dbg["ph1"] = nc.dram_tensor("dbg_ph1", [128, 4 * TQ + 32], F32, kind="ExternalOutput").ap()
        dbg["tb"] = nc.dram_tensor("dbg_tb", [128, T], F32, kind="ExternalOutput").ap()
        dbg["cco"] = nc.dram_tensor("dbg_cco", [128, 64], F32, kind="ExternalOutput").ap()
        dbg["canpad"] = nc.dram_tensor("dbg_canpad", [128, 512], F32, kind="ExternalOutput").ap()

    wkscr = nc.dram_tensor("wkscr", [C, C], BF16, kind="Internal").ap()
    wvscr = nc.dram_tensor("wvscr", [C, C], BF16, kind="Internal").ap()
    wrscr = nc.dram_tensor("wrscr", [C, C], BF16, kind="Internal").ap()
    woscr = nc.dram_tensor("woscr", [C, C], BF16, kind="Internal").ap()
    wq1scr = nc.dram_tensor("wq1scr", [FF, C], BF16, kind="Internal").ap()
    wq2scr = nc.dram_tensor("wq2scr", [C, FF], BF16, kind="Internal").ap()
    rscr = nc.dram_tensor("rscr", [8, 128, T], BF16, kind="Internal").ap()
    x2scr = nc.dram_tensor("x2scr", [NG, 128, C], F32, kind="Internal").ap()
    gpart_d = nc.dram_tensor("gpart", [1, 256], F32, kind="Internal").ap()
    gpart_v = gpart_d.rearrange("one (p two) -> p (one two)", p=128)
    ginv_d = nc.dram_tensor("ginv", [1, 2], F32, kind="Internal").ap()
    ccin_d = nc.dram_tensor("ccin", [1, 1024], F32, kind="Internal").ap()
    ccout_d = nc.dram_tensor("ccout", [8, 1024], F32, kind="Internal").ap()
    ccin_v = ccin_d.rearrange("one (p k) -> p (one k)", p=128)

    xg_d = x_d.rearrange("(g p) c -> g p c", p=128)
    outg_d = out_d.rearrange("(g p) c -> g p c", p=128)
    w1_rb = w1_d.rearrange("(a b) c -> a (b c)", a=128)
    w2_rb = w2_d.rearrange("(a b) c -> a (b c)", a=128)
    wq1_rb = wq1scr.rearrange("(a b) c -> a (b c)", a=128)
    wq2_rb = wq2scr.rearrange("(a b) c -> a (b c)", a=128)
    cco_v = ccout_d.rearrange("r (p k) -> p r k", p=128)

    p = Prog()
    p.dbg = dbg
    try:
        return _build_body(nc, p, stage, locals())
    except _Trunc:
        for e in Prog.ENGINES:
            for sem in p.sems.values():
                if sem.n:
                    p.wait(e, sem, sem.n)
        _realize(nc, p)
        return nc


def _build_body(nc, p, stage, dr):
    for k, v in dr.items():
        globals()
    ccin_v = dr["ccin_v"]
    (x_d, wk_d, wv_d, wr_d, wo_d, w1_d, w2_d, b1_d, b2_d, tdec_d, tfir_d,
     gam_d, bet_d, tvec_d, mask_d, cvec_d, idb_d, out_d, wkscr, wvscr, wrscr,
     woscr, wq1scr, wq2scr, rscr, x2scr, gpart_d, gpart_v, ginv_d, ccin_d,
     ccout_d, xg_d, outg_d, w1_rb, w2_rb, wq1_rb, wq2_rb, cco_v) = (
        dr["x_d"], dr["wk_d"], dr["wv_d"], dr["wr_d"], dr["wo_d"], dr["w1_d"],
        dr["w2_d"], dr["b1_d"], dr["b2_d"], dr["tdec_d"], dr["tfir_d"],
        dr["gam_d"], dr["bet_d"], dr["tvec_d"], dr["mask_d"], dr["cvec_d"],
        dr["idb_d"], dr["out_d"], dr["wkscr"], dr["wvscr"], dr["wrscr"],
        dr["woscr"], dr["wq1scr"], dr["wq2scr"], dr["rscr"], dr["x2scr"],
        dr["gpart_d"], dr["gpart_v"], dr["ginv_d"], dr["ccin_d"],
        dr["ccout_d"], dr["xg_d"], dr["outg_d"], dr["w1_rb"], dr["w2_rb"],
        dr["wq1_rb"], dr["wq2_rb"], dr["cco_v"])

    with ExitStack() as top:
        ent = top.enter_context

        sems = {}
        for name in ("s_x", "s_wa", "s_wf", "s_tr", "s_pe", "s_act", "s_dve",
                     "s_pool", "s_sp", "s_rr", "s_cc", "s_bar", "s_out",
                     "s_spx2", "s_spq", "s_spr", "s_coll"):
            sems[name] = Sem(ent(nc.semaphore(name)))
        p.sems = sems
        p.dve_sem = sems["s_dve"]
        s_x = sems["s_x"]; s_wa = sems["s_wa"]; s_wf = sems["s_wf"]
        s_tr = sems["s_tr"]; s_pe = sems["s_pe"]; s_act = sems["s_act"]
        s_dve = sems["s_dve"]; s_pool = sems["s_pool"]; s_sp = sems["s_sp"]
        s_rr = sems["s_rr"]; s_cc = sems["s_cc"]; s_bar = sems["s_bar"]
        s_out = sems["s_out"]; s_spx2 = sems["s_spx2"]; s_spq = sems["s_spq"]
        s_spr = sems["s_spr"]; s_coll = sems["s_coll"]

        tb = ent(nc.sbuf_tensor("tb", [128, T], F32))
        b2b = ent(nc.sbuf_tensor("b2b", [128, C], F32))
        cc = ent(nc.sbuf_tensor("cc", [128, 12], F32))
        mcol = ent(nc.sbuf_tensor("mcol", [128, 8], F32))
        idb = ent(nc.sbuf_tensor("idb_s", [128, 128], BF16))
        gamc = ent(nc.sbuf_tensor("gamc", [128, 8], F32))
        betc = ent(nc.sbuf_tensor("betc", [128, 8], F32))
        tdzc = ent(nc.sbuf_tensor("tdzc", [128, 8], F32))
        tfzc = ent(nc.sbuf_tensor("tfzc", [128, 8], F32))
        tdc_ = ent(nc.sbuf_tensor("tdc_", [128, 8], F32))
        tfc_ = ent(nc.sbuf_tensor("tfc_", [128, 8], F32))
        lntd = ent(nc.sbuf_tensor("lntd", [128, 8], F32))
        tmp8a = ent(nc.sbuf_tensor("tmp8a", [128, 8], F32))
        tmp8b = ent(nc.sbuf_tensor("tmp8b", [128, 8], F32))
        b1c = ent(nc.sbuf_tensor("b1c", [128, 32], F32))
        g1col = ent(nc.sbuf_tensor("g1col", [128, 1], F32))
        g2col = ent(nc.sbuf_tensor("g2col", [128, 1], F32))
        gth1n = ent(nc.sbuf_tensor("gth1n", [128, 1], F32))
        gth2n = ent(nc.sbuf_tensor("gth2n", [128, 1], F32))
        s1cols = ent(nc.sbuf_tensor("s1cols", [128, 64], F32))
        gred = ent(nc.sbuf_tensor("gred", [128, 8], F32))

        carry = ent(nc.sbuf_tensor("carry", [128, 16], F32))
        tdcar = ent(nc.sbuf_tensor("tdcar", [128, 8], F32))
        gsb = ent(nc.sbuf_tensor("gsb", [128, 64], F32))
        r3c = ent(nc.sbuf_tensor("r3c", [128, NG], F32))
        n3c = ent(nc.sbuf_tensor("n3c", [128, NG], F32))
        colt = ent(nc.sbuf_tensor("colt", [128, 64], F32))

        col_rearr = lambda ap: ap.rearrange("one (k q) -> q (one k)", q=128)
        p.dma("sync", s_x, tb[:, :], tvec_d.broadcast_to([128, T]))
        p.dma("sync", s_x, b2b[:, :], b2_d.broadcast_to([128, C]))
        p.dma("sync", s_x, cc[:, :], cvec_d.broadcast_to([128, 12]))
        p.dma("sync", s_x, mcol[:, :], mask_d.broadcast_to([128, 8]))
        p.dma("sync", s_x, idb[:, :], idb_d)
        p.dma("sync", s_x, gamc[:, :], col_rearr(gam_d),
              allow_slow_non_contiguous=True)
        p.dma("sync", s_x, betc[:, :], col_rearr(bet_d),
              allow_slow_non_contiguous=True)
        p.dma("sync", s_x, tdzc[:, :], col_rearr(tdec_d),
              allow_slow_non_contiguous=True)
        p.dma("sync", s_x, tfzc[:, :], col_rearr(tfir_d),
              allow_slow_non_contiguous=True)
        m_setup = p.dma("sync", s_x, b1c[:, :],
                        b1_d.rearrange("one (k q) -> q (one k)", q=128),
                        allow_slow_non_contiguous=True)

        p.wait("scalar", s_x, m_setup)
        p.op("scalar", s_act, lambda E: E.activation(
            tmp8a[:, :], tdzc[:, :], AF.Exp, scale=cc[:, 3:4]))
        ma = p.op("scalar", s_act, lambda E: E.activation(
            tmp8b[:, :], tfzc[:, :], AF.Exp, scale=cc[:, 3:4]))
        p.wait("scalar", s_act, ma)
        p.op("scalar", s_act, lambda E: E.activation(
            tmp8a[:, :], tmp8a[:, :], AF.Ln, bias=cc[:, 2:3]))
        mb = p.op("scalar", s_act, lambda E: E.activation(
            tmp8b[:, :], tmp8b[:, :], AF.Ln, bias=cc[:, 2:3]))
        p.wait("vector", s_act, mb)
        p.op("vector", s_dve, lambda E: E.tensor_scalar(
            lntd[:, :], tmp8a[:, :], -1.0, 0.0, ALU.mult, ALU.add))
        mtf = p.op("vector", s_dve, lambda E: E.tensor_scalar(
            tmp8b[:, :], tmp8b[:, :], -1.0, 0.0, ALU.mult, ALU.add))
        p.wait("scalar", s_dve, mtf)
        p.op("scalar", s_act, lambda E: E.activation(
            tdc_[:, :], lntd[:, :], AF.Exp))
        mtd = p.op("scalar", s_act, lambda E: E.activation(
            tfc_[:, :], tmp8b[:, :], AF.Exp))


        mid = ExitStack()
        partial = mid.enter_context(nc.sbuf_tensor("partial", [128, 8 * T], F32))
        with ExitStack() as attn_scope:
            aent = attn_scope.enter_context
            wkT = aent(nc.sbuf_tensor("wkT", [128, 8 * C], BF16))
            wvT = aent(nc.sbuf_tensor("wvT", [128, 8 * C], BF16))
            wrT = aent(nc.sbuf_tensor("wrT", [128, 8 * C], BF16))
            xnT = aent(nc.sbuf_tensor("xnT", [128, 8 * TQ], BF16))
            xsb = [aent(nc.sbuf_tensor(f"xsb{i}", [128, C], F32)) for i in range(2)]
            usb = [aent(nc.sbuf_tensor(f"usb{i}", [128, C], BF16)) for i in range(2)]
            scrb = aent(nc.sbuf_tensor("scrb", [128, C], BF16))
            kvb = aent(nc.sbuf_tensor("kvb", [128, TQ], F32))
            ksb = aent(nc.sbuf_tensor("ksb", [128, TQ], F32))
            wdk = aent(nc.sbuf_tensor("wdk", [128, TQ], F32))
            wkvb = aent(nc.sbuf_tensor("wkvb", [128, TQ], F32))
            sexct = aent(nc.sbuf_tensor("sexct", [128, TQ], F32))
            sexcs = aent(nc.sbuf_tensor("sexcs", [128, 8 * TQ], F32))
            tds = aent(nc.sbuf_tensor("tds", [128, TQ], F32))
            zer = aent(nc.sbuf_tensor("zer", [128, TQ], F32))
            rsb = [aent(nc.sbuf_tensor("rsb0", [128, TQ], BF16))] * 2
            wstg = aent(nc.sbuf_tensor("stage", [128, 4096], BF16))
            fst32 = aent(nc.sbuf_tensor("fst32", [128, 2048], F32))
            q1 = aent(nc.sbuf_tensor("q1", [128, 2048], BF16))
            q2 = aent(nc.sbuf_tensor("q2", [128, 2048], BF16))
            qq = aent(nc.sbuf_tensor("qq", [128, 2048], BF16))
            ps_tr = [aent(nc.psum_tensor(f"ps_tr{i}", [128, C], BF16)) for i in range(2)]
            ps_k = aent(nc.psum_tensor("ps_k", [128, TQ], F32))
            ps_v = aent(nc.psum_tensor("ps_v", [128, TQ], F32))
            ps_r = aent(nc.psum_tensor("ps_r", [128, TQ], F32))

            p.op0("vector", lambda E: E.memset(zer[:, :], 0.0))

            awspill = {}
            for wi, (wsrc, wscr) in enumerate(
                    [(wk_d, wkscr), (wv_d, wvscr), (wr_d, wrscr), (wo_d, woscr)]):
                wsrc_rb = wsrc.rearrange("(a b) c -> a (b c)", a=128)
                wscr_rb = wscr.rearrange("(a b) c -> a (b c)", a=128)
                for h in range(2):
                    if not (wi == 0 and h == 0):
                        p.wait("gpsimd", s_wa, s_wa.n)
                    p.dma("gpsimd", s_wf, wstg[:, :],
                          wsrc_rb[:, h * 4096:(h + 1) * 4096])
                    p.wait("scalar", s_wf, s_wf.n)
                    awspill[wi] = p.dma("scalar", s_wa,
                                        wscr_rb[:, h * 4096:(h + 1) * 4096],
                                        wstg[:, :])
            for wi, (wscr, wT) in enumerate([(wkscr, wkT), (wvscr, wvT), (wrscr, wrT)]):
                p.wait("sync", s_wa, awspill[wi])
                for c in range(8):
                    p.dma("sync", s_tr, wT[:, c * C:(c + 1) * C],
                          wscr[:, c * 128:(c + 1) * 128], transpose=True)
            m_wT = s_tr.n

            s1d1 = nc.dram_tensor("s1scr1", [32, 128], F32, kind="Internal").ap()
            s1d2 = nc.dram_tensor("s1scr2", [32, 128], F32, kind="Internal").ap()
            s2w_d = nc.dram_tensor("s2scr", [8, 1], F32, kind="Internal").ap()
            thr_d = nc.dram_tensor("thrscr", [1, 4], F32, kind="Internal").ap()
            scano = sexcs[:, 0:1024]
            absb = sexcs[:, 1024:2048]
            l2buf = sexcs[:, 2048:3072]
            absm = {}
            nslab = 0
            p.op0("vector", lambda E: E.memset(l2buf[:, :], 0.0))
            for wi in range(2):
                if wi == 1:
                    p.op0("vector", lambda E: E.memset(l2buf[:, :], 0.0))
                s1d = [s1d1, s1d2][wi]
                for t in range(32):
                    half = nslab % 2
                    if nslab >= 2:
                        p.wait("sync", s_dve, absm[nslab - 2])
                    if wi == 0:
                        for iq in range(4):
                            mld = p.dma(
                                "sync", s_wf,
                                fst32[32 * iq:32 * (iq + 1),
                                      half * 1024:(half + 1) * 1024].rearrange(
                                    "j (a c) -> j a c", c=32),
                                w1_d[128 * t + 32 * iq:128 * t + 32 * (iq + 1),
                                     :].rearrange("a (j c) -> j a c", c=32),
                                allow_slow_non_contiguous=True)
                    else:
                        mld = p.dma(
                            "sync", s_wf,
                            fst32[:, half * 1024:(half + 1) * 1024].rearrange(
                                "p (a c) -> p a c", c=32),
                            w2_d[32 * t:32 * (t + 1), :].rearrange(
                                "a (j c) -> j a c", c=32),
                            allow_slow_non_contiguous=True)
                    p.wait("scalar", s_wf, mld)
                    if nslab >= 1:
                        p.wait("scalar", s_dve, absm[nslab - 1])
                    mab = p.op("scalar", s_act, lambda E, h=half: E.activation(
                        absb[:, :], fst32[:, h * 1024:(h + 1) * 1024], AF.Abs))
                    p.wait("vector", s_act, mab)
                    absm[nslab] = p.op("vector", s_dve, lambda E: E.tensor_tensor_scan(
                        scano[:, :], l2buf[:, :], absb[:, :], 0.0,
                        ALU.add, ALU.add))
                    p.op("vector", s_dve, lambda E, n=nslab: E.tensor_copy(
                        s1cols[:, n:n + 1], scano[:, 1023:1024]))
                    nslab += 1
                mza = p.op("vector", s_dve, lambda E: E.memset(absb[:, :], 0.0))
                mdr = p.op("vector", s_dve, lambda E: E.memset(absb[:, 0:8], 0.0))
                p.wait("scalar", s_dve, mdr)
                cs1 = p.dma("scalar", s_cc, s1d.rearrange("n p -> p n"),
                            s1cols[:, 32 * wi:32 * (wi + 1)],
                            allow_slow_non_contiguous=True)
                p.wait("scalar", s_cc, cs1)
                if wi == 0:
                    ml2 = p.dma("scalar", s_cc, l2buf[0:4, :],
                                s1d1.rearrange("(k a) p -> k (a p)", k=4))
                else:
                    ml2 = p.dma("scalar", s_cc,
                                l2buf[0:4, :].rearrange("k (i jj) -> k i jj",
                                                        jj=32),
                                s1d2.rearrange("i (k jj) -> k i jj", k=4))
                p.wait("vector", s_cc, ml2)
                p.wait("vector", s_dve, mza)
                m2s = p.op("vector", s_dve, lambda E: E.tensor_tensor_scan(
                    scano[0:4, :], absb[0:4, :], l2buf[0:4, :], 0.0,
                    ALU.add, ALU.add))
                p.op("vector", s_dve, lambda E: E.tensor_copy(
                    gred[0:4, 7:8], scano[0:4, 1023:1024]))
                mdr2 = p.op("vector", s_dve, lambda E: E.memset(absb[:, 0:8], 0.0))
                p.wait("scalar", s_dve, mdr2)
                ms2 = p.dma("scalar", s_cc, s2w_d[4 * wi:4 * (wi + 1), :],
                            gred[0:4, 7:8])
                p.wait("scalar", s_cc, ms2)
                mfl = p.dma("scalar", s_cc, gred[0:1, 0:4],
                            s2w_d.rearrange("(a k) one -> a (k one)",
                                            a=2)[wi:wi + 1, :])
                p.wait("vector", s_cc, mfl)
                p.op("vector", s_dve, lambda E: E.tensor_tensor_scan(
                    scano[0:1, 0:4], absb[0:1, 0:4], gred[0:1, 0:4], 0.0,
                    ALU.add, ALU.add))
                if wi == 1:
                    p.wait("vector", s_cc, mthr)
                p.op("vector", s_dve, lambda E: E.tensor_scalar(
                    gred[0:1, 4:5], scano[0:1, 3:4], float(2.0 ** -22),
                    float(EPS), ALU.mult, ALU.add))
                p.op("vector", s_dve, lambda E: E.tensor_scalar(
                    gred[0:1, 5:6], gred[0:1, 4:5], 0.5, 0.0,
                    ALU.mult, ALU.add))
                mth = p.op("vector", s_dve, lambda E: E.tensor_scalar(
                    gred[0:1, 6:7], gred[0:1, 4:5], -0.5, 0.0,
                    ALU.mult, ALU.add))
                mdr3 = p.op("vector", s_dve, lambda E: E.memset(absb[:, 0:8], 0.0))
                p.wait("scalar", s_dve, mdr3)
                mthr = p.dma("scalar", s_cc, thr_d[:, 2 * wi:2 * wi + 2],
                             gred[0:1, 5:7])
            p.wait("scalar", s_cc, mthr)
            p.dma("scalar", s_cc, g1col[:, :],
                  thr_d[:, 0:1].broadcast_to([128, 1]))
            p.dma("scalar", s_cc, gth1n[:, :],
                  thr_d[:, 1:2].broadcast_to([128, 1]))
            p.dma("scalar", s_cc, g2col[:, :],
                  thr_d[:, 2:3].broadcast_to([128, 1]))
            m_ginv = p.dma("scalar", s_cc, gth2n[:, :],
                           thr_d[:, 3:4].broadcast_to([128, 1]))
            mzer = p.op("vector", s_dve, lambda E: E.memset(sexcs[:, :], 0.0))

            xld = {}
            xuse = {}
            xstat = {}
            trmark = {}
            evmark = {}
            umark = {}
            pemark_q = {}
            kevac = {}
            revac = {}
            rspill = {}
            kvmark = {}
            wmark = {}
            qspill = {}
            sgnmark = {}
            qqmark = {}

            if stage >= 2:
                for g in range(2):
                    xld[g] = p.dma("sync", s_x, xsb[g % 2][:, :], xg_d[g, :, :])

            nw_slab = 0
            for q in range(NQ if stage >= 2 else 0):
                qg0 = q * 4
                for gl in range(4):
                    g = qg0 + gl
                    par = g % 2
                    xt = xsb[par]
                    p.wait("vector", s_x, xld[g])
                    if g >= 1:
                        p.wait("vector", s_act, umark[g - 1])
                    xstat[g] = p.op("vector", s_dve, lambda E, xt=xt: E.tensor_reduce(
                        colt[:, 0:1], xt[:, :], AX.X, ALU.add))
                    p.op("vector", s_dve, lambda E, xt=xt: E.scalar_tensor_tensor(
                        scrb[:, :], xt[:, :], 1.0, xt[:, :], ALU.mult, ALU.mult))
                    p.op("vector", s_dve, lambda E: E.tensor_reduce(
                        colt[:, 4:5], scrb[:, :], AX.X, ALU.add))
                    p.op("vector", s_dve, lambda E: E.tensor_scalar(
                        colt[:, 8:9], colt[:, 0:1], 1.0 / C, 0.0, ALU.mult, ALU.add))
                    p.op("vector", s_dve, lambda E: E.tensor_tensor(
                        colt[:, 12:13], colt[:, 8:9], colt[:, 8:9], ALU.mult))
                    mvar = p.op("vector", s_dve, lambda E: E.scalar_tensor_tensor(
                        colt[:, 16:17], colt[:, 4:5], 1.0 / C, colt[:, 12:13],
                        ALU.mult, ALU.subtract))
                    p.wait("scalar", s_dve, mvar)
                    mln = p.op("scalar", s_act, lambda E: E.activation(
                        colt[:, 20:21], colt[:, 16:17], AF.Ln,
                        scale=cc[:, 4:5], bias=cc[:, 5:6]))
                    p.wait("scalar", s_act, mln)
                    mr1 = p.op("scalar", s_act, lambda E: E.activation(
                        colt[:, 24:25], colt[:, 20:21], AF.Exp, scale=cc[:, 0:1]))
                    mnm = p.op("vector", s_dve, lambda E: E.tensor_scalar(
                        colt[:, 28:29], colt[:, 8:9], -1.0, 0.0,
                        ALU.mult, ALU.add))
                    xcm = p.op("vector", s_dve, lambda E, xt=xt: E.tensor_scalar(
                        scrb[:, :], xt[:, :], colt[:, 28:29], 0.0,
                        ALU.add, ALU.add))
                    if g + 2 < NG:
                        p.wait("sync", s_dve, xcm)
                        xld[g + 2] = p.dma("sync", s_x, xsb[par][:, :],
                                           xg_d[g + 2, :, :])
                    p.wait("scalar", s_act, mr1)
                    p.wait("scalar", s_dve, xcm)
                    if g >= 2:
                        p.wait("scalar", s_pe, trmark[g - 2])
                    umark[g] = p.op("scalar", s_act, lambda E, par=par: E.activation(
                        usb[par][:, :], scrb[:, :], AF.Copy, scale=colt[:, 24:25]))
                    p.wait("tensor", s_act, umark[g])
                    p.wait("tensor", s_x, m_setup)
                    if g >= 2:
                        p.wait("tensor", s_dve, evmark[g - 2])
                    for c in range(8):
                        last = p.op("tensor", s_pe, lambda E, c=c, par=par: E.transpose(
                            ps_tr[par][:, c * 128:(c + 1) * 128],
                            usb[par][:, c * 128:(c + 1) * 128], idb[:, :]))
                    trmark[g] = last
                    p.wait("vector", s_pe, trmark[g])
                    for c in range(8):
                        last = p.op("vector", s_dve, lambda E, c=c, par=par, gl=gl: E.tensor_scalar(
                            xnT[:, c * TQ + gl * 128:c * TQ + (gl + 1) * 128],
                            ps_tr[par][:, c * 128:(c + 1) * 128],
                            gamc[:, c:c + 1], betc[:, c:c + 1], ALU.mult, ALU.add))
                    evmark[g] = last

                p.wait("tensor", s_dve, evmark[qg0 + 3])
                if q == 0:
                    p.wait("tensor", s_tr, m_wT)
                for d in range(8):
                    if (q, d) != (0, 0):
                        pq, pd = (q, d - 1) if d > 0 else (q - 1, 7)
                        p.wait("tensor", s_act, revac[(pq, pd)])
                        p.wait("tensor", s_dve, kvmark[(pq, pd)])
                    for wT, ps in ((wkT, ps_k), (wvT, ps_v), (wrT, ps_r)):
                        for c in range(8):
                            mmfn = (lambda E, c=c, d=d, wT=wT, ps=ps: E.matmul(
                                ps[:, 0:TQ],
                                wT[:, c * C + d * 128:c * C + (d + 1) * 128],
                                xnT[:, c * TQ:c * TQ + TQ],
                                start=(c == 0), stop=(c == 7)))
                            if wT is wrT and c == 7:
                                mpe = p.op("tensor", s_pe, mmfn)
                            else:
                                p.op0("tensor", mmfn)
                    p.wait("scalar", s_pe, mpe)
                    kevac[(q, d)] = p.op("scalar", s_act, lambda E: E.copy(
                        ksb[:, :], ps_k[:, :]))
                    wmark[(q, d)] = p.op("scalar", s_act, lambda E, d=d, q=q: E.activation(
                        wdk[:, :], tb[:, q * TQ:(q + 1) * TQ], AF.Exp,
                        scale=lntd[:, d:d + 1]))
                    rpar = 0
                    if (q * 8 + d) >= 1:
                        p.wait("scalar", s_spr, rspill[divmod(q * 8 + d - 1, 8)])
                    revac[(q, d)] = p.op("scalar", s_act, lambda E, rpar=rpar: E.copy(
                        rsb[rpar][:, :], ps_r[:, :]))
                    p.wait("scalar", s_act, revac[(q, d)])
                    rspill[(q, d)] = p.dma("scalar", s_spr,
                                           rscr[d, :, q * TQ:(q + 1) * TQ],
                                           rsb[rpar][:, :])
                    p.wait("vector", s_pe, mpe)
                    p.wait("vector", s_act, kevac[(q, d)])
                    kvmark[(q, d)] = p.op("vector", s_dve, lambda E: E.tensor_tensor(
                        kvb[:, :], ksb[:, :], ps_v[:, :], ALU.mult))
                    p.wait("vector", s_act, wmark[(q, d)])
                    mwkv = p.op("vector", s_dve, lambda E: E.tensor_tensor(
                        wkvb[:, :], kvb[:, :], wdk[:, :], ALU.mult))
                    p.wait("vector", s_dve, mwkv)
                    sl = slice(d * TQ, (d + 1) * TQ)
                    ini = 0.0 if q == 0 else sexcs[:, (d + 1) * TQ - 1:(d + 1) * TQ]
                    mscan = p.op("vector", s_dve, lambda E, ini=ini: E.tensor_tensor_scan(
                        sexct[:, :], zer[:, :], wkvb[:, :], ini,
                        ALU.add, ALU.add))
                    p.wait("vector", s_dve, mscan)
                    p.op0("vector", lambda E, sl=sl: E.tensor_copy(
                        sexcs[:, sl], sexct[:, :]))
                    p.op0("vector", lambda E: E.scalar_tensor_tensor(
                        tds[:, :], wkvb[:, :], -1.0, sexct[:, :],
                        ALU.mult, ALU.add))
                    p.op0("vector", lambda E, d=d: E.tensor_scalar(
                        wkvb[:, :], tds[:, :], tdc_[:, d:d + 1], 0.0,
                        ALU.mult, ALU.add))
                    mpart = p.op("vector", s_dve, lambda E, d=d, q=q: E.scalar_tensor_tensor(
                        partial[:, d * T + q * TQ:d * T + (q + 1) * TQ],
                        kvb[:, :], tfc_[:, d:d + 1], wkvb[:, :],
                        ALU.mult, ALU.add))
                    if DEBUG and q == 0 and d == 0:
                        p.wait("sync", s_dve, mpart)
                        p.dma("sync", s_out, p.dbg["ph1"][:, 0:TQ], kvb[:, :])
                        p.dma("gpsimd", s_out, p.dbg["ph1"][:, TQ:TQ + 16],
                              xnT[:, 0:16])
                        p.dma("gpsimd", s_out, p.dbg["cols"][:, 0:8],
                              usb[0][:, 0:8])
                        p.dma("gpsimd", s_out, p.dbg["cols"][:, 8:40],
                              colt[:, 0:32])
                        p.dma("gpsimd", s_out, p.dbg["cols"][:, 40:48],
                              xsb[0][:, 0:8])
                        p.dma("sync", s_out, p.dbg["ph1"][:, 2 * TQ:3 * TQ],
                              sexct[:, :])
                        p.dma("sync", s_out, p.dbg["ph1"][:, 3 * TQ:4 * TQ],
                              sexcs[:, 0:TQ])
                        p.wait("vector", s_out, s_out.n)

                for _ in range(0 if SKIP_PASSB else 8):
                    wi, s = divmod(nw_slab, 16)
                    nw_slab += 1
                    wrb = [w1_rb, w2_rb][wi]
                    qrb = [wq1_rb, wq2_rb][wi]
                    gcol = [g1col, g2col][wi]
                    if nw_slab > 1:
                        p.wait("sync", s_act, sgnmark[nw_slab - 2])
                    if nw_slab == 1:
                        p.wait("sync", s_dve, absm[63])
                    mld = p.dma("sync", s_rr, fst32[:, :],
                                wrb[:, (s * 2048):(s + 1) * 2048])
                    if PASSB_LEVEL < 2:
                        sgnmark[nw_slab - 1] = s_act.n
                        qqmark[nw_slab - 1] = s_pool.n
                        qspill[nw_slab - 1] = s_spq.n
                        continue
                    p.wait("scalar", s_rr, mld)
                    p.wait("scalar", s_cc, m_ginv)
                    if nw_slab > 1:
                        p.wait("scalar", s_pool, qqmark[nw_slab - 2])
                    if SIGN_AS_COPY:
                        p.op("scalar", s_act, lambda E, gcol=gcol: E.activation(
                            q1[:, :], fst32[:, :], AF.Copy, scale=gcol[:, 0:1]))
                        sgnmark[nw_slab - 1] = p.op("scalar", s_act,
                                                    lambda E, gcol=gcol: E.activation(
                            q2[:, :], fst32[:, :], AF.Copy, scale=gcol[:, 0:1]))
                    else:
                        ncol = [gth1n, gth2n][wi]
                        p.op("scalar", s_act, lambda E, ncol=ncol: E.activation(
                            q1[:, :], fst32[:, :], AF.Sign,
                            scale=cc[:, 2:3], bias=ncol[:, 0:1]))
                        sgnmark[nw_slab - 1] = p.op("scalar", s_act,
                                                    lambda E, gcol=gcol: E.activation(
                            q2[:, :], fst32[:, :], AF.Sign,
                            scale=cc[:, 2:3], bias=gcol[:, 0:1]))
                    if PASSB_LEVEL < 3:
                        qqmark[nw_slab - 1] = s_pool.n
                        qspill[nw_slab - 1] = s_spq.n
                        continue
                    p.wait("vector", s_act, sgnmark[nw_slab - 1])
                    if nw_slab > 1:
                        p.wait("vector", s_spq, qspill[nw_slab - 2])
                    qqmark[nw_slab - 1] = p.op("vector", s_pool,
                                               lambda E: E.tensor_tensor(
                        qq[:, :], q1[:, :], q2[:, :], ALU.add))
                    if PASSB_LEVEL < 4:
                        qspill[nw_slab - 1] = s_spq.n
                        continue
                    p.wait("scalar", s_pool, qqmark[nw_slab - 1])
                    qspill[nw_slab - 1] = p.dma("scalar", s_spq,
                                                qrb[:, (s * 2048):(s + 1) * 2048],
                                                qq[:, :])

            if stage < 3:
                raise _Trunc()
            mstate = s_dve.n
            p.wait("gpsimd", s_dve, mstate)
            mcc1 = p.dma("gpsimd", s_cc, ccin_v,
                         sexcs[:, :].rearrange("p (d t) -> p d t", t=TQ)[:, :, TQ - 1],
                         allow_slow_non_contiguous=True)
            p.wait("gpsimd", s_cc, mcc1)
            mcc2 = p.op("gpsimd", s_coll, lambda E: E.collective_compute(
                "AllGather", ALU.bypass, replica_groups=[list(range(NCORE))],
                ins=[ccin_d], outs=[ccout_d]))
            p.wait("gpsimd", s_coll, mcc2)
            mcc3 = p.dma("gpsimd", s_cc,
                         gsb[:, :].rearrange("p (r k) -> p r k", k=8), cco_v)
            p.wait("vector", s_cc, mcc3)
            p.op0("vector", lambda E: E.memset(carry[:, 0:8], 0.0))
            for r in range(8):
                src, dst = (0, 8) if r % 2 == 0 else (8, 0)
                p.op0("vector", lambda E, r=r, src=src, dst=dst: E.scalar_tensor_tensor(
                    carry[:, dst:dst + 8], gsb[:, r * 8:(r + 1) * 8],
                    mcol[:, r:r + 1], carry[:, src:src + 8], ALU.mult, ALU.add))
            m_tdc = p.op("vector", s_dve, lambda E: E.tensor_tensor(
                tdcar[:, :], tdc_[:, :], carry[:, 8:16], ALU.mult))

            p.wait("sync", s_x, s_x.n)
            p.wait("sync", s_rr, s_rr.n)
            p.wait("sync", s_tr, s_tr.n)
            p.wait("scalar", s_spr, s_spr.n)
            p.wait("scalar", s_spq, s_spq.n)
            p.wait("scalar", s_wa, s_wa.n)
            p.wait("scalar", s_act, s_act.n)
            p.wait("vector", s_dve, s_dve.n)
            p.wait("tensor", s_pe, s_pe.n)
            p.wait("gpsimd", s_wf, s_wf.n)
            p.wait("gpsimd", s_cc, s_cc.n)
            p.wait("gpsimd", s_pool, s_pool.n)
            for e in Prog.ENGINES:
                p.op(e, s_bar, (lambda E: E.nop()))
            bar1 = s_bar.n
            for e in Prog.ENGINES:
                p.wait(e, s_bar, bar1)

        with ExitStack() as a2:
            aent = a2.enter_context
            zfm = aent(nc.sbuf_tensor("zfm", [128, 8 * T], BF16))
            woT = aent(nc.sbuf_tensor("woT", [128, 8 * C], BF16))
            rre = [aent(nc.sbuf_tensor(f"rre{i}", [128, T], BF16)) for i in range(2)]
            thb = [aent(nc.sbuf_tensor(f"thb{i}", [128, T], BF16)) for i in range(2)]
            qp = aent(nc.sbuf_tensor("qp", [128, T], F32))
            xsb2 = [aent(nc.sbuf_tensor(f"xsb2{i}", [128, C], F32)) for i in range(2)]
            aosb = [aent(nc.sbuf_tensor(f"aosb{i}", [128, C], F32)) for i in range(2)]
            u2sb = aent(nc.sbuf_tensor("u2sb", [128, C], F32))
            x2sb = [aent(nc.sbuf_tensor(f"x2sb{i}", [128, C], F32)) for i in range(2)]
            scb2 = aent(nc.sbuf_tensor("scb2", [128, C], BF16))
            scf2 = aent(nc.sbuf_tensor("scf2", [128, C], F32))
            ps_ao = [aent(nc.psum_tensor(f"ps_ao{i}", [128, C], F32)) for i in range(2)]

            if stage < 4:
                raise _Trunc()
            p.wait("sync", s_wa, awspill[3])
            for c in range(8):
                p.dma("sync", s_tr, woT[:, c * C:(c + 1) * C],
                      woscr[:, c * 128:(c + 1) * 128], transpose=True)
            m_woT = s_tr.n

            if DEBUG:
                p.wait("sync", s_dve, m_tdc)
                p.dma("sync", s_out,
                      p.dbg["cco"].rearrange("p (r k) -> p r k", k=8),
                      ccout_d.rearrange("r (p k) -> p r k", p=128))
                p.dma("sync", s_out, p.dbg["carry"][:, 0:64], gsb[:, :])
                p.dma("sync", s_out, p.dbg["carry"][:, 64:80], carry[:, :])
                p.dma("sync", s_out, p.dbg["carry"][:, 80:88], tdcar[:, :])
                p.dma("sync", s_out, p.dbg["carry"][:, 88:90], gred[:, 6:8])
                p.dma("sync", s_out, p.dbg["part"], partial[:, :])

            zmark = {}
            rld = {}
            thm = {}
            for d in range(2):
                rld[d] = p.dma("sync", s_spr, rre[d][:, :], rscr[d, :, :])
            for d in range(8):
                par = d % 2
                p.wait("scalar", s_spr, rld[d])
                if d >= 2:
                    p.wait("scalar", s_dve, zmark[d - 2])
                thm[d] = p.op("scalar", s_act, lambda E, par=par: E.activation(
                    thb[par][:, :], rre[par][:, :], AF.Tanh, scale=cc[:, 1:2]))
                if d + 2 < 8:
                    p.wait("sync", s_act, thm[d])
                    rld[d + 2] = p.dma("sync", s_spr, rre[par][:, :],
                                       rscr[d + 2, :, :])
                p.wait("vector", s_dve, m_tdc)
                p.wait("vector", s_act, thm[d])
                p.op0("vector", lambda E, d=d: E.tensor_scalar(
                    qp[:, :], partial[:, d * T:(d + 1) * T],
                    tdcar[:, d:d + 1], 0.5, ALU.add, ALU.mult))
                zmark[d] = p.op("vector", s_dve, lambda E, d=d, par=par: E.scalar_tensor_tensor(
                    zfm[:, d * T:(d + 1) * T], thb[par][:, :], 1.0,
                    qp[:, :], ALU.add, ALU.mult))
            m_zall = zmark[7]

            if DEBUG:
                p.wait("sync", s_dve, m_zall)
                p.dma("sync", s_out, p.dbg["z"], zfm[:, :])

            xld2 = {}
            aoev = {}
            u2m = {}
            x2m = {}
            x2sp = {}
            sq2m = {}
            st3m = {}
            for g in range(2):
                xld2[g] = p.dma("sync", s_x, xsb2[g % 2][:, :], xg_d[g, :, :])
            p.wait("tensor", s_bar, bar1)
            p.wait("tensor", s_dve, m_zall)
            p.wait("tensor", s_tr, m_woT)
            for g in range(NG):
                par = g % 2
                if g >= 2:
                    p.wait("tensor", s_dve, aoev[g - 2])
                for d in range(8):
                    for h in range(2):
                        mmfn = (lambda E, d=d, h=h, g=g, par=par: E.matmul(
                            ps_ao[par][:, h * 512:(h + 1) * 512],
                            zfm[:, d * T + g * 128:d * T + (g + 1) * 128],
                            woT[:, d * C + h * 512:d * C + (h + 1) * 512],
                            start=(d == 0), stop=(d == 7)))
                        if d == 7 and h == 1:
                            mpe2 = p.op("tensor", s_pe, mmfn)
                        else:
                            p.op0("tensor", mmfn)
                p.wait("vector", s_pe, mpe2)
                if g >= 2:
                    p.wait("vector", s_act, u2m[g - 2])
                aoev[g] = p.op("vector", s_dve, lambda E, par=par: E.tensor_scalar(
                    aosb[par][:, :], ps_ao[par][:, :], 1.0, 0.0, ALU.mult,
                    ALU.add))
                p.op("vector", s_dve, lambda E, par=par: E.tensor_reduce(
                    colt[:, 0:1], aosb[par][:, :], AX.X, ALU.add))
                sq2m[g] = p.op("vector", s_dve, lambda E, par=par: E.scalar_tensor_tensor(
                    scf2[:, :], aosb[par][:, :], 1.0, aosb[par][:, :],
                    ALU.mult, ALU.mult))
                p.op("vector", s_dve, lambda E: E.tensor_reduce(
                    colt[:, 4:5], scf2[:, :], AX.X, ALU.add))
                p.op0("vector", lambda E: E.tensor_scalar(
                    colt[:, 8:9], colt[:, 0:1], 1.0 / C, 0.0, ALU.mult, ALU.add))
                p.op0("vector", lambda E: E.tensor_tensor(
                    colt[:, 12:13], colt[:, 8:9], colt[:, 8:9], ALU.mult))
                mvar = p.op("vector", s_dve, lambda E: E.scalar_tensor_tensor(
                    colt[:, 16:17], colt[:, 4:5], 1.0 / C, colt[:, 12:13],
                    ALU.mult, ALU.subtract))
                p.wait("scalar", s_dve, mvar)
                mln = p.op("scalar", s_act, lambda E: E.activation(
                    colt[:, 20:21], colt[:, 16:17], AF.Ln, bias=cc[:, 6:7]))
                p.wait("scalar", s_act, mln)
                mr2 = p.op("scalar", s_act, lambda E: E.activation(
                    colt[:, 24:25], colt[:, 20:21], AF.Exp, scale=cc[:, 0:1]))
                mnm2 = p.op("vector", s_dve, lambda E: E.tensor_scalar(
                    colt[:, 28:29], colt[:, 8:9], -1.0, 0.0, ALU.mult, ALU.add))
                if g >= 1:
                    p.wait("vector", s_act, u2m[g - 1])
                xc2m = p.op("vector", s_dve, lambda E, par=par: E.tensor_scalar(
                    u2sb[:, :], aosb[par][:, :], colt[:, 28:29], 0.0,
                    ALU.add, ALU.add))
                p.wait("scalar", s_act, mr2)
                p.wait("scalar", s_dve, xc2m)
                u2m[g] = p.op("scalar", s_act, lambda E: E.activation(
                    scf2[:, :], u2sb[:, :], AF.Copy, scale=colt[:, 24:25]))
                p.wait("vector", s_x, xld2[g])
                p.wait("vector", s_act, u2m[g])
                if g >= 2:
                    p.wait("vector", s_spx2, x2sp[g - 2])
                x2m[g] = p.op("vector", s_dve, lambda E, par=par: E.tensor_tensor(
                    x2sb[par][:, :], scf2[:, :], xsb2[par][:, :], ALU.add))
                if g + 2 < NG:
                    p.wait("sync", s_dve, x2m[g])
                    xld2[g + 2] = p.dma("sync", s_x, xsb2[par][:, :],
                                        xg_d[g + 2, :, :])
                st3a = p.op("vector", s_dve, lambda E, par=par: E.tensor_reduce(
                    colt[:, 32:33], x2sb[par][:, :], AX.X, ALU.add))
                st3b = p.op("vector", s_dve, lambda E, par=par: E.scalar_tensor_tensor(
                    scf2[:, :], x2sb[par][:, :], 1.0, x2sb[par][:, :],
                    ALU.mult, ALU.mult))
                p.op("vector", s_dve, lambda E: E.tensor_reduce(
                    colt[:, 36:37], scf2[:, :], AX.X, ALU.add))
                p.op0("vector", lambda E: E.tensor_scalar(
                    colt[:, 40:41], colt[:, 32:33], 1.0 / C, 0.0, ALU.mult, ALU.add))
                p.op0("vector", lambda E: E.tensor_tensor(
                    colt[:, 44:45], colt[:, 40:41], colt[:, 40:41], ALU.mult))
                mv3 = p.op("vector", s_dve, lambda E: E.scalar_tensor_tensor(
                    colt[:, 48:49], colt[:, 36:37], 1.0 / C, colt[:, 44:45],
                    ALU.mult, ALU.subtract))
                p.wait("scalar", s_dve, mv3)
                ml3 = p.op("scalar", s_act, lambda E: E.activation(
                    colt[:, 52:53], colt[:, 48:49], AF.Ln, bias=cc[:, 6:7]))
                p.wait("scalar", s_act, ml3)
                mr3 = p.op("scalar", s_act, lambda E, g=g: E.activation(
                    r3c[:, g:g + 1], colt[:, 52:53], AF.Exp, scale=cc[:, 0:1]))
                st3m[g] = p.op("vector", s_dve, lambda E, g=g: E.tensor_scalar(
                    n3c[:, g:g + 1], colt[:, 40:41], -1.0, 0.0,
                    ALU.mult, ALU.add))
                p.wait("sync", s_dve, x2m[g])
                x2sp[g] = p.dma("sync", s_spx2, x2scr[g, :, :], x2sb[par][:, :])

            p.wait("sync", s_spx2, s_spx2.n)
            p.wait("sync", s_x, s_x.n)
            p.wait("sync", s_spr, s_spr.n)
            p.wait("sync", s_tr, s_tr.n)
            p.wait("scalar", s_act, s_act.n)
            p.wait("vector", s_dve, s_dve.n)
            p.wait("tensor", s_pe, s_pe.n)
            p.wait("gpsimd", s_cc, s_cc.n)
            for e in Prog.ENGINES:
                p.op(e, s_bar, (lambda E: E.nop()))
            bar2 = s_bar.n
            for e in Prog.ENGINES:
                p.wait(e, s_bar, bar2)
            if DEBUG_X2:
                p.dma("sync", s_out,
                      p.dbg["x2"].rearrange("(g p) c -> g p c", p=128), x2scr)

        mid.close()

        if stage < 5:
            raise _Trunc()

        with ExitStack() as b2s:
            bent = b2s.enter_context
            wq1T = bent(nc.sbuf_tensor("wq1T", [128, 8 * FF], BF16))
            wq2T = bent(nc.sbuf_tensor("wq2T", [128, 32 * C], BF16))
            hsb = bent(nc.sbuf_tensor("hsb", [128, 32 * 256], BF16))
            l3T = bent(nc.sbuf_tensor("l3T", [128, 8 * 256], BF16))
            x2r = [bent(nc.sbuf_tensor(f"x2r{i}", [128, C], F32)) for i in range(2)]
            xc3b = bent(nc.sbuf_tensor("xc3b", [128, C], F32))
            u3sb = [bent(nc.sbuf_tensor(f"u3sb{i}", [128, C], BF16)) for i in range(2)]
            ysb = [bent(nc.sbuf_tensor(f"ysb{i}", [128, C], F32)) for i in range(2)]
            u4sb = bent(nc.sbuf_tensor("u4sb", [128, C], F32))
            osb = [bent(nc.sbuf_tensor(f"osb{i}", [128, C], F32)) for i in range(2)]
            scf3 = bent(nc.sbuf_tensor("scf3", [128, C], F32))
            ps_tr2 = bent(nc.psum_tensor("ps_tr2", [128, C], BF16))
            ps_h = [bent(nc.psum_tensor(f"ps_h{i}", [128, 256], F32)) for i in range(2)]
            ps_y = [bent(nc.psum_tensor(f"ps_y{i}", [128, C], F32)) for i in range(2)]

            if not SKIP_PASSB:
                p.wait("sync", s_spq, qspill[15])
            for c in range(8):
                p.dma("sync", s_tr, wq1T[:, c * FF:(c + 1) * FF],
                      wq1scr[:, c * 128:(c + 1) * 128], transpose=True)
            m_w1T = s_tr.n
            if not SKIP_PASSB:
                p.wait("sync", s_spq, qspill[31])
            for f in range(32):
                p.dma("sync", s_tr, wq2T[:, f * C:(f + 1) * C],
                      wq2scr[:, f * 128:(f + 1) * 128], transpose=True)
            m_w2T = s_tr.n

            x2ld = {}
            u3m = {}
            tr3m = {}
            ev3m = {}
            hm = {}
            lastmph = {}
            lastmpy = {}
            yev = {}
            sq4m = {}
            u4m = {}
            outm = {}
            outsp = {}
            ffn_dmp = {}
            hdmp = {}
            for g in range(2):
                x2ld[g] = p.dma("sync", s_spx2, x2r[g % 2][:, :], x2scr[g, :, :])

            p.wait("tensor", s_bar, bar2)
            p.wait("tensor", s_tr, m_w1T)
            for t in range(8):
                for gl in range(2):
                    g = 2 * t + gl
                    par = g % 2
                    p.wait("vector", s_spx2, x2ld[g])
                    p.wait("vector", s_dve, st3m[g])
                    if g >= 1:
                        p.wait("vector", s_act, u3m[g - 1])
                    xc3m = p.op("vector", s_dve, lambda E, par=par, g=g: E.tensor_scalar(
                        xc3b[:, :], x2r[par][:, :], n3c[:, g:g + 1], 0.0,
                        ALU.add, ALU.add))
                    p.wait("scalar", s_dve, xc3m)
                    if g >= 2:
                        p.wait("scalar", s_pe, tr3m[g - 2])
                    u3m[g] = p.op("scalar", s_act, lambda E, par=par, g=g: E.activation(
                        u3sb[par][:, :], xc3b[:, :], AF.Copy,
                        scale=r3c[:, g:g + 1]))
                    p.wait("tensor", s_act, u3m[g])
                    if t > 0 and gl == 0:
                        p.wait("tensor", s_dve, ev3m[g - 1])
                    elif gl == 1:
                        p.wait("tensor", s_dve, ev3m[g - 1])
                    for c in range(8):
                        last = p.op("tensor", s_pe, lambda E, c=c, par=par: E.transpose(
                            ps_tr2[:, c * 128:(c + 1) * 128],
                            u3sb[par][:, c * 128:(c + 1) * 128], idb[:, :]))
                    tr3m[g] = last
                    p.wait("vector", s_pe, tr3m[g])
                    if t > 0 and gl == 0:
                        p.wait("vector", s_pe, lastmph[t - 1])
                    for c in range(8):
                        last = p.op("vector", s_dve, lambda E, c=c, gl=gl: E.tensor_copy(
                            l3T[:, c * 256 + gl * 128:c * 256 + (gl + 1) * 128],
                            ps_tr2[:, c * 128:(c + 1) * 128]))
                    ev3m[g] = last
                p.wait("tensor", s_dve, ev3m[2 * t + 1])
                relum = {}
                for f in range(32):
                    fp = f % 2
                    if f >= 2:
                        p.wait("tensor", s_act, relum[f - 2])
                    if t > 0 and f == 0:
                        p.wait("scalar", s_pe, lastmpy[t - 1])
                        if DEBUG_FFN:
                            p.wait("scalar", s_out, hdmp[t - 1])
                    for c in range(8):
                        mmfn = (lambda E, c=c, f=f, fp=fp: E.matmul(
                            ps_h[fp][:, :],
                            wq1T[:, c * FF + f * 128:c * FF + (f + 1) * 128],
                            l3T[:, c * 256:(c + 1) * 256],
                            start=(c == 0), stop=(c == 7)))
                        if c == 7:
                            mph = p.op("tensor", s_pe, mmfn)
                        else:
                            p.op0("tensor", mmfn)
                    p.wait("scalar", s_pe, mph)
                    relum[f] = p.op("scalar", s_act, lambda E, f=f, fp=fp: E.activation(
                        hsb[:, f * 256:(f + 1) * 256], ps_h[fp][:, :], AF.Relu,
                        scale=cc[:, 1:2], bias=b1c[:, f:f + 1]))
                hm[t] = relum[31]
                lastmph[t] = mph
                if DEBUG_FFN:
                    p.wait("sync", s_act, hm[t])
                    hdmp[t] = p.dma("sync", s_out,
                                    p.dbg["h"][:, t * 8192:(t + 1) * 8192],
                                    hsb[:, :])
                p.wait("tensor", s_act, hm[t])
                if t == 0:
                    p.wait("tensor", s_tr, m_w2T)
                for gl in range(2):
                    g = 2 * t + gl
                    par = g % 2
                    if t > 0:
                        p.wait("tensor", s_dve, yev[(t - 1, gl)])
                    for f in range(32):
                        for h in range(2):
                            mmfn = (lambda E, f=f, h=h, gl=gl, par=par: E.matmul(
                                ps_y[par][:, h * 512:(h + 1) * 512],
                                hsb[:, f * 256 + gl * 128:f * 256 + (gl + 1) * 128],
                                wq2T[:, f * C + h * 512:f * C + (h + 1) * 512],
                                start=(f == 0), stop=(f == 31)))
                            if f == 31 and h == 1:
                                mpy = p.op("tensor", s_pe, mmfn)
                            else:
                                p.op0("tensor", mmfn)
                    p.wait("vector", s_pe, mpy)
                    if g >= 2:
                        p.wait("vector", s_act, u4m[g - 2])
                        if DEBUG_FFN:
                            p.wait("vector", s_out, ffn_dmp[g - 2])
                    yev[(t, gl)] = p.op("vector", s_dve, lambda E, par=par: E.scalar_tensor_tensor(
                        ysb[par][:, :], ps_y[par][:, :], 0.5, b2b[:, :],
                        ALU.mult, ALU.add))
                    p.op("vector", s_dve, lambda E, par=par: E.tensor_reduce(
                        colt[:, 0:1], ysb[par][:, :], AX.X, ALU.add))
                    sq4m[g] = p.op("vector", s_dve, lambda E, par=par: E.scalar_tensor_tensor(
                        scf3[:, :], ysb[par][:, :], 1.0, ysb[par][:, :],
                        ALU.mult, ALU.mult))
                    p.op("vector", s_dve, lambda E: E.tensor_reduce(
                        colt[:, 4:5], scf3[:, :], AX.X, ALU.add))
                    p.op0("vector", lambda E: E.tensor_scalar(
                        colt[:, 8:9], colt[:, 0:1], 1.0 / C, 0.0, ALU.mult, ALU.add))
                    p.op0("vector", lambda E: E.tensor_tensor(
                        colt[:, 12:13], colt[:, 8:9], colt[:, 8:9], ALU.mult))
                    mv4 = p.op("vector", s_dve, lambda E: E.scalar_tensor_tensor(
                        colt[:, 16:17], colt[:, 4:5], 1.0 / C, colt[:, 12:13],
                        ALU.mult, ALU.subtract))
                    p.wait("scalar", s_dve, mv4)
                    ml4 = p.op("scalar", s_act, lambda E: E.activation(
                        colt[:, 20:21], colt[:, 16:17], AF.Ln, bias=cc[:, 6:7]))
                    p.wait("scalar", s_act, ml4)
                    mr4 = p.op("scalar", s_act, lambda E: E.activation(
                        colt[:, 24:25], colt[:, 20:21], AF.Exp, scale=cc[:, 0:1]))
                    p.wait("vector", s_act, mr4)
                    mnb4 = p.op("vector", s_dve, lambda E: E.scalar_tensor_tensor(
                        colt[:, 28:29], colt[:, 8:9], -1.0, colt[:, 24:25],
                        ALU.mult, ALU.mult))
                    p.wait("scalar", s_dve, mnb4)
                    if DEBUG_FFN and g >= 1:
                        p.wait("scalar", s_out, ffn_dmp[g - 1])
                    u4m[g] = p.op("scalar", s_act, lambda E, par=par: E.activation(
                        u4sb[:, :], ysb[par][:, :], AF.Copy, scale=colt[:, 24:25]))
                    p.wait("vector", s_act, u4m[g])
                    if g >= 2:
                        p.wait("vector", s_out, outsp[g - 2])
                    outm[g] = p.op("vector", s_dve, lambda E, par=par: E.scalar_tensor_tensor(
                        osb[par][:, :], x2r[par][:, :], colt[:, 28:29],
                        u4sb[:, :], ALU.add, ALU.add))
                    p.wait("sync", s_dve, outm[g])
                    outsp[g] = p.dma("sync", s_out, outg_d[g, :, :], osb[par][:, :])
                    if DEBUG_FFN:
                        p.dma("sync", s_out, p.dbg["y"][:, g * C:(g + 1) * C],
                              ysb[par][:, :])
                        p.dma("sync", s_out, p.dbg["x2r"][:, g * C:(g + 1) * C],
                              x2r[par][:, :])
                        ffn_dmp[g] = p.dma("sync", s_out,
                                           p.dbg["u4"][:, g * C:(g + 1) * C],
                                           u4sb[:, :])
                    lastmpy[t] = mpy
                for gl in range(2):
                    g2 = 2 * (t + 1) + gl
                    if g2 < NG:
                        p.wait("sync", s_dve, outm[2 * t + gl])
                        if DEBUG_FFN:
                            p.wait("sync", s_out, ffn_dmp[2 * t + gl])
                        x2ld[g2] = p.dma("sync", s_spx2, x2r[gl][:, :],
                                         x2scr[g2, :, :])

            if DEBUG_FFN:
                p.dma("sync", s_out, p.dbg["wq2"], wq2T[:, :])
                p.dma("sync", s_out, p.dbg["wq1"], wq1T[:, :])
                p.dma("sync", s_out, p.dbg["thr"], thr_d)
            p.wait("sync", s_out, s_out.n)

        _realize(nc, p)
    return nc


def _realize(nc, p):
        with nc.Block() as block:
            block.sync(lambda E: [f(E) for f in p.q["sync"]])
            block.scalar(lambda E: [f(E) for f in p.q["scalar"]])
            block.vector(lambda E: [f(E) for f in p.q["vector"]])
            block.tensor(lambda E: [f(E) for f in p.q["tensor"]])
            block.gpsimd(lambda E: [f(E) for f in p.q["gpsimd"]])


_CACHED = {}


def kernel(L0_attn_x, L0_attn_time_decay, L0_attn_time_first, L0_attn_Wk,
           L0_attn_Wv, L0_attn_Wr, L0_attn_Wo, L0_attn_ln_gamma,
           L0_attn_ln_beta, L0_ffn_W1, L0_ffn_b1, L0_ffn_W2, L0_ffn_b2):
    x = np.ascontiguousarray(np.asarray(L0_attn_x, np.float32).reshape(8 * T, C))
    f32 = lambda a, shape: np.ascontiguousarray(
        np.asarray(a, np.float32).reshape(shape))
    common = {
        "wk": f32(L0_attn_Wk, (C, C)), "wv": f32(L0_attn_Wv, (C, C)),
        "wr": f32(L0_attn_Wr, (C, C)), "wo": f32(L0_attn_Wo, (C, C)),
        "w1": f32(L0_ffn_W1, (FF, C)), "w2": f32(L0_ffn_W2, (C, FF)),
        "b1": f32(L0_ffn_b1, (1, FF)), "b2": f32(L0_ffn_b2, (1, C)),
        "tdec": f32(L0_attn_time_decay, (1, C)),
        "tfir": f32(L0_attn_time_first, (1, C)),
        "gam": f32(L0_attn_ln_gamma, (1, C)),
        "bet": f32(L0_attn_ln_beta, (1, C)),
        "cvec": np.array([[-0.5, 0.5, 1.0, -1.0, 1.0 + EPS, EPS * EPS,
                           EPS, 0.0, np.log(float(NW)), 0.0, 0.0, 0.0]],
                         np.float32),
        "idb": np.eye(128, dtype=ml_dtypes.bfloat16),
    }
    in_maps = []
    for i in range(NCORE):
        m = dict(common)
        m["x"] = x[i * T:(i + 1) * T]
        m["tvec"] = np.arange(i * T, (i + 1) * T, dtype=np.float32).reshape(1, T)
        m["mask"] = (np.arange(8) < i).astype(np.float32).reshape(1, 8)
        in_maps.append(m)

    if "nc" not in _CACHED:
        _CACHED["nc"] = build_program()
    res = run_bass_kernel_spmd(_CACHED["nc"], in_maps, core_ids=list(range(NCORE)))
    out = np.concatenate([np.asarray(r["out"]) for r in res.results], axis=0)
    return out.reshape(1, 8 * T, C)
